# version 10
# baseline (speedup 1.0000x reference)
import numpy as np
import ml_dtypes
from contextlib import ExitStack
import concourse.bass as bass
import concourse.mybir as mybir
from concourse.bass_utils import run_bass_kernel_spmd

F32 = mybir.dt.float32
BF16 = mybir.dt.bfloat16
U8 = mybir.dt.uint8
AF = mybir.ActivationFunctionType
ALU = mybir.AluOpType
AX = mybir.AxisListType
bfdt = ml_dtypes.bfloat16

NCORES = 8
B, S, H = 16, 2048, 2048
BLOC = B // NCORES
CH = 512
NCH = BLOC * S // CH
HQ, HKV, D = 32, 4, 64
NF = 8192
KB = 1024
VBLK = 386
INV2048 = 1.0 / 2048.0


class Sem:
    def __init__(self, h):
        self.h = h
        self.v = 0

    def inc(self, amt=1):
        self.v += amt
        return self.v


class Em:
    ENGS = ("tensor", "vector", "scalar", "sync", "gpsimd")

    def __init__(self):
        self.q = {e: [] for e in self.ENGS}

    def wait(self, eng, tok):
        if tok is None:
            return
        sem, val = tok
        if val <= 0:
            return
        assert val <= sem.v, f"wait on future token {val} > {sem.v}"
        self.q[eng].append(lambda e, h=sem.h, v=val: e.wait_ge(h, v))

    def op(self, eng, fn, inc=None, amt=1):
        if inc is None:
            self.q[eng].append(fn)
            return None
        val = inc.inc(amt)

        def run(e, f=fn, h=inc.h, a=amt):
            f(e).then_inc(h, a)
        self.q[eng].append(run)
        return (inc, val)


def build_program():
    nc = bass.Bass("TRN2", target_bir_lowering=False)

    x_in = nc.dram_tensor("x", [BLOC, S, H], F32, kind="ExternalInput")
    wqkv = nc.dram_tensor("wqkv", [2560, H], F32, kind="ExternalInput")
    wo = nc.dram_tensor("wo", [H, H], F32, kind="ExternalInput")
    w0 = nc.dram_tensor("w0", [NF, H], F32, kind="ExternalInput")
    b0 = nc.dram_tensor("b0", [NF], F32, kind="ExternalInput")
    w1 = nc.dram_tensor("w1", [H, NF], F32, kind="ExternalInput")
    b1 = nc.dram_tensor("b1", [H], F32, kind="ExternalInput")
    t_cos = nc.dram_tensor("tcos", [128, S], BF16, kind="ExternalInput")
    t_sin = nc.dram_tensor("tsin", [128, S], BF16, kind="ExternalInput")
    t_p128 = nc.dram_tensor("tp128", [128, 128], BF16, kind="ExternalInput")
    t_trim = nc.dram_tensor("ttrim", [128, 128], BF16, kind="ExternalInput")
    t_idf = nc.dram_tensor("tidf", [64, 64], F32, kind="ExternalInput")
    t_idb = nc.dram_tensor("tidb", [128, 128], BF16, kind="ExternalInput")
    y_out = nc.dram_tensor("y", [BLOC, S, H], F32, kind="ExternalOutput")

    wqkv_bf = nc.dram_tensor("wqkv_bf", [2560, H], BF16, kind="Internal")
    wo_bf = nc.dram_tensor("wo_bf", [H, H], BF16, kind="Internal")
    w0_bf = nc.dram_tensor("w0_bf", [NF, H], BF16, kind="Internal")
    w1_bf = nc.dram_tensor("w1_bf", [H, NF], BF16, kind="Internal")
    wqkvT = nc.dram_tensor("wqkvT", [H, 2560], BF16, kind="Internal")
    woT = nc.dram_tensor("woT", [H, H], BF16, kind="Internal")
    w0T = nc.dram_tensor("w0T", [H, NF], BF16, kind="Internal")
    w1T = nc.dram_tensor("w1T", [NF, H], BF16, kind="Internal")
    xn_dram = nc.dram_tensor("xn_dram", [2, CH, H], BF16, kind="Internal")

    ctx = ExitStack()
    sb = lambda name, shape, dt: ctx.enter_context(nc.sbuf_tensor(name, list(shape), dt))

    xnfm = sb("xnfm", [128, 16 * 512], BF16)
    s_ch = sb("s_ch", [128, 4 * 2048], BF16)
    kcache = sb("kcache", [128, 2 * 2048], BF16)
    vcache = sb("vcache", [128, 16 * VBLK], BF16)
    wslab = sb("wslab", [128, 4 * 4096], BF16)
    w2r = sb("w2r", [128, 2 * 2048], BF16)
    cosf = sb("cosf", [128, S], BF16)
    sinf = sb("sinf", [128, S], BF16)
    p128 = sb("p128", [128, 128], BF16)
    trim = sb("trim", [128, 128], BF16)
    idf = sb("idf", [64, 64], F32)
    idb = sb("idb", [128, 128], BF16)
    b0sb = sb("b0sb", [64, 128], F32)
    b0col = sb("b0col", [128, 64], F32)
    b1r = sb("b1r", [1, H], BF16)
    onesr = sb("onesr", [1, 512], BF16)
    kbst = sb("kbst", [128, 4], F32)
    lnst = sb("lnst", [128, 24], F32)

    arena_items = []
    ARENA = 96 * KB

    def place(name, shape, dt, off):
        nbytes = int(np.prod(shape[1:])) * mybir.dt.size(dt)
        assert off + nbytes <= ARENA, f"arena overflow {name}"
        h = nc.virtual_sbuf_tensor(name, list(shape), dt)
        arena_items.append((h, off))
        return h

    Qfm = [place(f"qfm{i}", [128, 512], BF16, i * KB) for i in range(16)]
    attnfm = [place(f"afm{i}", [128, 512], BF16, 16 * KB + i * KB) for i in range(16)]
    x_t = [place(f"xt{i}", [128, 2048], F32, 32 * KB + i * 8 * KB) for i in range(2)]
    sqs = place("sqs", [128, 2048], F32, 48 * KB)
    xn_tm = [place(f"xntm{i}", [128, 2048], BF16, 56 * KB + i * 4 * KB) for i in range(2)]
    G = [place(f"g{i}", [128, 512], BF16, i * KB) for i in range(64)]
    Qs = [place(f"qs{i}", [128, 512], BF16, 64 * KB + i * KB) for i in range(2)]
    t1b = [place(f"t1b{i}", [128, 512], F32, 66 * KB + i * 2 * KB) for i in range(2)]
    probsT = [[place(f"pt{h}_{i}", [128, 512], BF16, 64 * KB + (2 * h + i) * KB)
               for i in range(2)] for h in range(2)]
    avrawA = [place(f"avA{i}", [65, 512], BF16, 70 * KB + i * KB) for i in range(2)]
    avrawB = [place(f"avB{i}", [128, 512], BF16, 72 * KB + i * KB) for i in range(2)]
    recfA = [place(f"rfA{i}", [1, 512], F32, 74 * KB + i * 2 * KB) for i in range(2)]
    recfB = [place(f"rfB{i}", [1, 512], F32, 78 * KB + i * 2 * KB) for i in range(2)]
    recbA = [place(f"rbA{i}", [1, 512], BF16, 82 * KB + i * KB) for i in range(2)]
    recbB = [place(f"rbB{i}", [1, 512], BF16, 84 * KB + i * KB) for i in range(2)]
    gp = [place(f"gp{i}", [128, 512], BF16, 86 * KB + i * KB) for i in range(2)]
    xp = [place(f"xp{i}", [128, 512], F32, 88 * KB + i * 2 * KB) for i in range(2)]
    tp = [place(f"tp{i}", [128, 512], F32, 92 * KB + i * 2 * KB) for i in range(2)]

    arena = ctx.enter_context(nc.sbuf_tensor("arena", [128, ARENA], U8))
    base = nc.lookup_mloc(arena).addr
    for h, off in arena_items:
        nc._tensor(h.name, h.shape, h.dtype, type="SB")
        m = nc.lookup_mloc(h)
        m.allocated = True
        m.addr = base + off

    PS = [ctx.enter_context(nc.psum_tensor(f"ps{i}", [128, 512], F32)) for i in range(8)]

    dS = Sem(ctx.enter_context(nc.semaphore("dS")))
    dA = Sem(ctx.enter_context(nc.semaphore("dA")))
    dP = Sem(ctx.enter_context(nc.semaphore("dP")))
    pe = Sem(ctx.enter_context(nc.semaphore("pe")))
    ve = Sem(ctx.enter_context(nc.semaphore("ve")))
    sc = Sem(ctx.enter_context(nc.semaphore("sc")))

    em = Em()

    em.op("vector", lambda e: e.memset(kbst[:, 0:1], -50.0))
    em.op("vector", lambda e: e.memset(kbst[:, 1:2], 1e-5))
    em.op("vector", lambda e: e.memset(onesr[:], 1.0))
    em.op("vector", lambda e: e.memset(vcache[:], 0.0))
    vc4 = vcache[:].rearrange("p (t h x) -> p t h x", t=16, h=2, x=193)
    em.op("vector", lambda e: e.memset(vc4[:, :, :, 64:65], 1.0))
    tok_vinit = em.op("vector", lambda e: e.memset(vc4[:, :, :, 65:66], 1.0), inc=ve)

    em.op("sync", lambda e: e.dma_start(cosf[:], t_cos.ap()), inc=dS, amt=16)
    em.op("sync", lambda e: e.dma_start(sinf[:], t_sin.ap()), inc=dS, amt=16)
    em.op("sync", lambda e: e.dma_start(p128[:], t_p128.ap()), inc=dS, amt=16)
    em.op("sync", lambda e: e.dma_start(trim[:], t_trim.ap()), inc=dS, amt=16)
    em.op("sync", lambda e: e.dma_start(idf[:], t_idf.ap()), inc=dS, amt=16)
    em.op("sync", lambda e: e.dma_start(idb[:], t_idb.ap()), inc=dS, amt=16)
    tok_tabs = em.op("sync", lambda e: e.dma_start(
        b0sb[:], b0.ap().rearrange("(a p) -> a p", p=128)), inc=dS, amt=16)

    em.wait("tensor", tok_tabs)
    tok_b0t = em.op("tensor", lambda e: e.transpose(PS[0][0:128, 0:64], b0sb[:], idf[:]), inc=pe)
    em.wait("vector", tok_b0t)
    tok_b0c = em.op("vector", lambda e: e.tensor_copy(b0col[:], PS[0][:, 0:64]), inc=ve)

    for g2 in range(2):
        for half in range(2):
            r0 = 1024 * g2 + 512 * half
            qsrc = wqkv.ap()[r0:r0 + 512, :].rearrange(
                "(p8 d) c -> p8 d c", p8=8, d=64)
            qdst = wqkv_bf.ap()[1024 * g2:1024 * (g2 + 1), :].rearrange(
                "(p8 half d) c -> half p8 d c", p8=8, half=2, d=64)[half]
            em.op("gpsimd", lambda e, qdst=qdst, qsrc=qsrc: e.dma_start(qdst, qsrc),
                  inc=dP, amt=16)
    tok_c_qkv = em.op("gpsimd", lambda e: e.dma_start(
        wqkv_bf.ap()[2048:2560, :], wqkv.ap()[2048:2560, :]), inc=dP, amt=16)
    tok_c_wo = em.op("gpsimd", lambda e: e.dma_start(wo_bf.ap(), wo.ap()), inc=dP, amt=16)
    tok_c_w0 = None
    for i in range(4):
        tok_c_w0 = em.op("gpsimd", lambda e, i=i: e.dma_start(
            w0_bf.ap()[2048 * i:2048 * (i + 1), :],
            w0.ap()[2048 * i:2048 * (i + 1), :]), inc=dP, amt=16)
    tok_c_w1 = None
    for i in range(4):
        tok_c_w1 = em.op("gpsimd", lambda e, i=i: e.dma_start(
            w1_bf.ap()[512 * i:512 * (i + 1), :],
            w1.ap()[512 * i:512 * (i + 1), :]), inc=dP, amt=16)
    em.op("gpsimd", lambda e: e.dma_start(b1r[:], b1.ap().unsqueeze(0)), inc=dP, amt=16)

    slab_state = [dict(free=None) for _ in range(4)]
    slab_rr = [0]

    def get_pair():
        s0 = slab_rr[0]
        if s0 + 2 > 4:
            s0 = 0
        slab_rr[0] = (s0 + 2) % 4
        em.wait("sync", slab_state[s0]["free"])
        em.wait("sync", slab_state[s0 + 1]["free"])
        return s0, wslab[:, 4096 * s0: 4096 * s0 + 8192]

    def prol_transpose(src_bf, dst_T, R, blocks, cast_tok, store_rows=None):
        last = None
        pending = []

        def do_store(i, s0, view, tok_ld):
            nonlocal last
            em.wait("sync", tok_ld)
            if store_rows is None:
                tok_st = em.op("sync", lambda e, i=i, view=view: e.dma_start(
                    dst_T.ap()[128 * i:128 * (i + 1), :], view[:, 0:R]),
                    inc=dS, amt=16)
            else:
                for (plo, phi, drow) in store_rows(i):
                    tok_st = em.op("sync", lambda e, view=view, plo=plo, phi=phi,
                                   drow=drow: e.dma_start(
                        dst_T.ap()[drow:drow + (phi - plo), :],
                        view[plo:phi, 0:R]), inc=dS, amt=16)
            slab_state[s0]["free"] = tok_st
            slab_state[s0 + 1]["free"] = tok_st
            last = tok_st

        for i in range(blocks):
            s0, view = get_pair()
            em.wait("sync", cast_tok)
            tok_ld = em.op("sync", lambda e, i=i, view=view: e.dma_start(
                view[:, 0:R], src_bf.ap()[:, 128 * i:128 * (i + 1)],
                transpose=True), inc=dS, amt=16)
            pending.append((i, s0, view, tok_ld))
            if len(pending) == 2:
                do_store(*pending.pop(0))
        while pending:
            do_store(*pending.pop(0))
        return last

    def wo_rows(i):
        out = []
        for k, hh in enumerate((2 * i, 2 * i + 1)):
            p = hh % 8 + 8 * (hh // 16)
            half = (hh // 8) % 2
            out.append((64 * k, 64 * (k + 1), p * 128 + half * 64))
        return out

    wready = {}

    w2slot = [w2r[:, 2048 * i:2048 * (i + 1)] for i in range(2)]
    w2_free = [None, None]

    bank_ready = [tok_b0t] + [None] * 7
    bank_free = [tok_b0c] + [None] * 7

    tok = {
        "xnfm_ready": None, "xnfm_free": None,
        "attn_pe_done": None, "ffn2_pe_done": None,
        "sch_ve_done": None, "sch_ready": None,
        "gelu_last": None, "rope_done": None, "v_done": None,
        "attn_ready_last": None, "out_last": None, "xnd_stored": None,
    }
    xt_free = [None, None]
    xntm_free = [None, None]
    xnd_free = [None, None]
    qs_free = [None, None]
    t1_free = [None, None]
    pt_free = [[None, None], [None, None]]
    av_free = [None, None]
    rec_free = [None, None]
    gp_free = [None, None]
    xp_free = [None, None]
    tp_free = [None, None]

    def load_slab(src_T, col0, wtok):
        s0, view = get_pair()
        em.wait("sync", wtok)
        src = src_T.ap()[:, col0:col0 + 512].rearrange(
            "(hc p) n -> hc p n", p=128).transpose([1, 0, 2])
        tok_ld = em.op("sync", lambda e, view=view, src=src: e.dma_start(view, src),
                       inc=dS, amt=16)
        tiles = [view[:, 512 * i: 512 * (i + 1)] for i in range(16)]
        return tiles, tok_ld, (s0, s0 + 1)

    def free_slab(slots, tk):
        for s in slots:
            slab_state[s]["free"] = tk

    def phase_A1(c):
        seq, cc = c // 4, c % 4
        row0 = cc * CH
        em.wait("scalar", tok["ffn2_pe_done"])
        em.wait("vector", tok["ffn2_pe_done"])
        last_store = None
        for half in range(2):
            tiles = (2 * half, 2 * half + 1)
            ve1 = {}
            for t in tiles:
                sl = t % 2
                em.wait("scalar", xt_free[sl])
                tok_x = em.op("scalar", lambda e, seq=seq, r=row0 + 128 * t, sl=sl:
                              e.dma_start(x_t[sl][:], x_in.ap()[seq, r:r + 128, :]),
                              inc=dA, amt=16)
                em.wait("vector", tok_x)
                em.op("vector", lambda e, sl=sl: e.tensor_tensor(
                    sqs[:], x_t[sl][:], x_t[sl][:], ALU.mult))
                em.op("vector", lambda e, t=t: e.reduce_sum(
                    lnst[:, 6 * t + 1:6 * t + 2], sqs[:], AX.X))
                ve1[t] = em.op("vector", lambda e, t=t, sl=sl: e.reduce_sum(
                    lnst[:, 6 * t + 0:6 * t + 1], x_t[sl][:], AX.X), inc=ve)
            sc1 = {}
            for t in tiles:
                em.wait("scalar", ve1[t])
                sc1[t] = em.op("scalar", lambda e, t=t: e.activation(
                    lnst[:, 6 * t + 2:6 * t + 3], lnst[:, 6 * t + 0:6 * t + 1],
                    AF.Square, scale=INV2048), inc=sc)
            ve2 = {}
            for t in tiles:
                em.wait("vector", sc1[t])
                ve2[t] = em.op("vector", lambda e, t=t: e.tensor_scalar(
                    lnst[:, 6 * t + 3:6 * t + 4], lnst[:, 6 * t + 1:6 * t + 2],
                    INV2048, lnst[:, 6 * t + 2:6 * t + 3], ALU.mult, ALU.subtract),
                    inc=ve)
            sc2 = {}
            for t in tiles:
                em.wait("scalar", ve2[t])
                em.op("scalar", lambda e, t=t: e.activation(
                    lnst[:, 6 * t + 4:6 * t + 5], lnst[:, 6 * t + 3:6 * t + 4],
                    AF.Ln, bias=kbst[:, 1:2]))
            for t in tiles:
                sc2[t] = em.op("scalar", lambda e, t=t: e.activation(
                    lnst[:, 6 * t + 4:6 * t + 5], lnst[:, 6 * t + 4:6 * t + 5],
                    AF.Exp, scale=-0.5), inc=sc)
            ve3 = {}
            for t in tiles:
                em.wait("vector", sc2[t])
                ve3[t] = em.op("vector", lambda e, t=t: e.tensor_scalar(
                    lnst[:, 6 * t + 5:6 * t + 6], lnst[:, 6 * t + 0:6 * t + 1],
                    lnst[:, 6 * t + 4:6 * t + 5], -INV2048, ALU.mult, ALU.mult), inc=ve)
            for t in tiles:
                sl = t % 2
                em.wait("scalar", ve3[t])
                em.wait("scalar", xntm_free[sl])
                tk = em.op("scalar", lambda e, t=t, sl=sl: e.activation(
                    xn_tm[sl][:], x_t[sl][:], AF.Identity,
                    bias=lnst[:, 6 * t + 5:6 * t + 6],
                    scale=lnst[:, 6 * t + 4:6 * t + 5]), inc=sc)
                xt_free[sl] = tk
                em.wait("tensor", tk)
                if t == 0:
                    em.wait("scalar", tok["xnfm_free"])
                tr_tok = None
                for grp in range(4):
                    rb = 2 + (grp % 2)
                    pbf = PS[rb][:].bitcast(BF16)
                    em.wait("tensor", bank_free[rb])
                    for i in range(4):
                        hc = 4 * grp + i
                        tr_tok = em.op("tensor", lambda e, sl=sl, hc=hc, i=i, pbf=pbf:
                                       e.transpose(
                            pbf[:, 128 * i:128 * (i + 1)],
                            xn_tm[sl][:, 128 * hc:128 * (hc + 1)], idb[:]),
                            inc=pe if i == 3 else None)
                    bank_ready[rb] = tr_tok
                    em.wait("scalar", tr_tok)
                    dst = xnfm[:].rearrange("p (hc n) -> p hc n", hc=16)[
                        :, 4 * grp:4 * grp + 4, 128 * t:128 * (t + 1)]
                    cp_tok = em.op("scalar", lambda e, dst=dst, pbf=pbf: e.copy(
                        dst, pbf[:, 0:512].rearrange("p (a n) -> p a n", a=4)), inc=sc)
                    bank_free[rb] = cp_tok
                xntm_free[sl] = tr_tok
                tok["xnfm_ready"] = cp_tok

    def phase_A2(c):
        pass

    def phase_B(c):
        seq, cc = c // 4, c % 4
        apd = tok["attn_pe_done"]
        em.wait("vector", apd)
        em.wait("vector", tok["ffn2_pe_done"])
        em.wait("scalar", apd)
        em.wait("vector", tok_vinit)
        csl = cosf[:, CH * cc: CH * cc + 512]
        ssl = sinf[:, CH * cc: CH * cc + 512]
        tok_v = None
        pend_rope = None

        def flush_rope(oc, sp, tok_qs):
            rbank = 2 + (oc % 2)
            em.wait("tensor", bank_free[rbank])
            em.wait("tensor", tok_qs)
            tok_rot = em.op("tensor", lambda e, sp=sp, rbank=rbank: e.matmul(
                PS[rbank][:], p128[:], Qs[sp][:], start=True, stop=True), inc=pe)
            bank_ready[rbank] = tok_rot
            if oc < 16:
                dest = Qfm[oc][:]
            else:
                kk2 = oc - 16
                dest = kcache[:, 2048 * kk2 + CH * cc: 2048 * kk2 + CH * cc + 512]
            em.wait("vector", tok_qs)
            em.wait("vector", t1_free[sp])
            em.op("vector", lambda e, sp=sp, csl=csl: e.tensor_tensor(
                t1b[sp][:], Qs[sp][:], csl, ALU.mult))
            em.wait("vector", tok_rot)
            em.op("vector", lambda e, sp=sp, ssl=ssl, rbank=rbank: e.tensor_tensor(
                Qs[sp][:], PS[rbank][:], ssl, ALU.mult))
            tk3 = em.op("vector", lambda e, sp=sp, dest=dest: e.tensor_tensor(
                dest, t1b[sp][:], Qs[sp][:], ALU.add), inc=ve)
            qs_free[sp] = tk3
            t1_free[sp] = tk3
            bank_free[rbank] = tk3
            if oc == 17:
                tok["rope_done"] = tk3
        for osl in range(5):
            tiles, tok_ld, slots = load_slab(wqkvT, 512 * osl, wready["qkv"])
            last_use = None
            for oc in range(4 * osl, min(4 * osl + 4, 18)):
                bank = oc % 2
                j = oc - 4 * osl
                em.wait("tensor", bank_free[bank])
                em.wait("tensor", tok_ld)
                em.wait("tensor", tok["xnfm_ready"])
                tok_mm = None
                for hc in range(16):
                    def fmm(e, hc=hc, j=j, tiles=tiles, bank=bank,
                            first=(hc == 0), lastm=(hc == 15)):
                        return e.matmul(PS[bank][:], tiles[hc][:, 128 * j:128 * (j + 1)],
                                        xnfm[:, 512 * hc:512 * (hc + 1)],
                                        start=first, stop=lastm)
                    if hc == 15:
                        tok_mm = em.op("tensor", fmm, inc=pe)
                    else:
                        em.op("tensor", fmm)
                bank_ready[bank] = tok_mm
                last_use = tok_mm
                sp = oc % 2
                em.wait("scalar", tok_mm)
                em.wait("scalar", qs_free[sp])
                tok_qs = em.op("scalar", lambda e, sp=sp, bank=bank: e.copy(
                    Qs[sp][:], PS[bank][:]), inc=sc)
                bank_free[bank] = tok_qs
                if pend_rope is not None:
                    flush_rope(*pend_rope)
                pend_rope = (oc, sp, tok_qs)
            if osl == 4:
                voff = [0, 65 + 64, 193, 258 + 64]
                for qt in range(4):
                    bank = qt % 2
                    em.wait("tensor", bank_free[bank])
                    tok_mm = None
                    for hc in range(16):
                        def fmm(e, hc=hc, qt=qt, tiles=tiles, bank=bank,
                                first=(hc == 0), lastm=(hc == 15)):
                            return e.matmul(
                                PS[bank][:, 0:256],
                                xnfm[:, 512 * hc + 128 * qt:512 * hc + 128 * (qt + 1)],
                                tiles[hc][:, 256:512], start=first, stop=lastm)
                        if hc == 15:
                            tok_mm = em.op("tensor", fmm, inc=pe)
                        else:
                            em.op("tensor", fmm)
                    bank_ready[bank] = tok_mm
                    last_use = tok_mm
                    em.wait("scalar", tok_mm)
                    for g in range(4):
                        tok_v = em.op("scalar", lambda e, g=g, bank=bank, kt=4 * cc + qt,
                                      voff=voff: e.copy(
                            vcache[:, VBLK * kt + voff[g]: VBLK * kt + voff[g] + 64],
                            PS[bank][:, 64 * g:64 * (g + 1)]), inc=sc)
                    bank_free[bank] = tok_v
            free_slab(slots, last_use)
        if pend_rope is not None:
            flush_rope(*pend_rope)
            pend_rope = None
        tok["v_done"] = tok_v

    def phase_C(c):
        seq, cc = c // 4, c % 4
        nkt = 4 * cc + 4
        tails = []

        def flush_tails(force):
            while (len(tails) >= 2) or (force and tails):
                j, p2, avA, avB, tok_ev = tails.pop(0)
                em.wait("vector", tok_ev)
                em.wait("vector", rec_free[p2])
                em.op("vector", lambda e, p2=p2: e.reciprocal(
                    recfA[p2][:], avrawA[p2][64:65, :]))
                em.op("vector", lambda e, p2=p2: e.reciprocal(
                    recfB[p2][:], avrawB[p2][0:1, :]))
                em.op("vector", lambda e, p2=p2: e.tensor_copy(recbA[p2][:], recfA[p2][:]))
                tok_rb = em.op("vector", lambda e, p2=p2: e.tensor_copy(
                    recbB[p2][:], recfB[p2][:]), inc=ve)
                em.wait("tensor", tok_rb)
                em.wait("tensor", bank_free[avA])
                em.wait("tensor", bank_free[avB])
                em.op("tensor", lambda e, p2=p2, avA=avA: e.matmul(
                    PS[avA][:], onesr[0:1, 0:128], recbA[p2][:], start=True, stop=True))
                tok_bc = em.op("tensor", lambda e, p2=p2, avB=avB: e.matmul(
                    PS[avB][:], onesr[0:1, 0:128], recbB[p2][:], start=True, stop=True),
                    inc=pe)
                bank_ready[avA] = tok_bc
                bank_ready[avB] = tok_bc
                tok["attn_pe_done"] = tok_bc
                em.wait("vector", tok_bc)
                em.op("vector", lambda e, j=j, p2=p2, avA=avA: e.tensor_tensor(
                    attnfm[j][0:64, :], avrawA[p2][0:64, :], PS[avA][0:64, :], ALU.mult))
                tok_nm = em.op("vector", lambda e, j=j, p2=p2, avB=avB: e.tensor_tensor(
                    attnfm[j][64:128, :], avrawB[p2][64:128, :], PS[avB][64:128, :],
                    ALU.mult), inc=ve)
                bank_free[avA] = tok_nm
                bank_free[avB] = tok_nm
                av_free[p2] = tok_nm
                rec_free[p2] = tok_nm
                tok["attn_ready_last"] = tok_nm

        for j in range(16):
            kk = j // 8
            p2 = j % 2
            avA, avB = 4 + 2 * p2, 5 + 2 * p2
            av_last = None
            pend = None

            def emit_av(kt, col0, tok_e, mask_tok, lastm, kk=kk, avA=avA, avB=avB):
                nonlocal av_last
                em.wait("tensor", tok_e)
                if mask_tok is not None:
                    em.wait("tensor", mask_tok)
                offA = VBLK * kt + (0 if kk == 0 else 193)
                offB = VBLK * kt + (65 if kk == 0 else 258)
                em.op("tensor", lambda e, kt=kt, col0=col0, offA=offA, avA=avA,
                      first=(kt == 0), lastm=lastm: e.matmul(
                    PS[avA][0:65, col0:512], vcache[:, offA:offA + 65],
                    probsT[0][kt % 2][:, col0:512], start=first, stop=lastm))
                tok_av = em.op("tensor", lambda e, kt=kt, col0=col0, offB=offB, avB=avB,
                               first=(kt == 0), lastm=lastm: e.matmul(
                    PS[avB][:, col0:512], vcache[:, offB:offB + 128],
                    probsT[1][kt % 2][:, col0:512], start=first, stop=lastm), inc=pe)
                pt_free[0][kt % 2] = tok_av
                pt_free[1][kt % 2] = tok_av
                av_last = tok_av

            for kt in range(nkt):
                dr = kt - 4 * cc
                col0 = max(dr, 0) * 128
                sbA, sbB = kt % 2, 2 + kt % 2
                em.wait("tensor", bank_free[sbA])
                em.wait("tensor", bank_free[sbB])
                if kt == 0:
                    if j == 0:
                        em.wait("tensor", tok["rope_done"])
                        em.wait("tensor", tok["v_done"])
                    em.wait("tensor", av_free[p2])
                    em.wait("tensor", bank_free[avA])
                    em.wait("tensor", bank_free[avB])
                em.op("tensor", lambda e, sbA=sbA, kk=kk, kt=kt, j=j, col0=col0: e.matmul(
                    PS[sbA][:, col0:512],
                    kcache[0:64, 2048 * kk + 128 * kt: 2048 * kk + 128 * (kt + 1)],
                    Qfm[j][0:64, col0:512], start=True, stop=True))
                tok_s = em.op("tensor", lambda e, sbB=sbB, kk=kk, kt=kt, j=j, col0=col0:
                              e.matmul(
                    PS[sbB][:, col0:512],
                    kcache[64:128, 2048 * kk + 128 * kt: 2048 * kk + 128 * (kt + 1)],
                    Qfm[j][64:128, col0:512], start=True, stop=True), inc=pe)
                bank_ready[sbA] = tok_s
                bank_ready[sbB] = tok_s
                em.wait("scalar", tok_s)
                em.wait("scalar", pt_free[0][kt % 2])
                em.wait("scalar", pt_free[1][kt % 2])
                em.op("scalar", lambda e, kt=kt, col0=col0, sbA=sbA: e.activation(
                    probsT[0][kt % 2][:, col0:512], PS[sbA][:, col0:512], AF.Exp,
                    bias=kbst[:, 0:1], scale=0.125))
                tok_e = em.op("scalar", lambda e, kt=kt, col0=col0, sbB=sbB: e.activation(
                    probsT[1][kt % 2][:, col0:512], PS[sbB][:, col0:512], AF.Exp,
                    bias=kbst[:, 0:1], scale=0.125), inc=sc)
                bank_free[sbA] = tok_e
                bank_free[sbB] = tok_e
                mask_tok = None
                if dr >= 0:
                    em.wait("vector", tok_e)
                    em.op("vector", lambda e, kt=kt, col0=col0: e.tensor_tensor(
                        probsT[0][kt % 2][:, col0:col0 + 128],
                        probsT[0][kt % 2][:, col0:col0 + 128], trim[:], ALU.mult))
                    mask_tok = em.op("vector", lambda e, kt=kt, col0=col0: e.tensor_tensor(
                        probsT[1][kt % 2][:, col0:col0 + 128],
                        probsT[1][kt % 2][:, col0:col0 + 128], trim[:], ALU.mult), inc=ve)
                if pend is not None:
                    emit_av(*pend, lastm=False)
                pend = (kt, col0, tok_e, mask_tok)
            emit_av(*pend, lastm=True)
            pend = None
            bank_ready[avA] = av_last
            bank_ready[avB] = av_last
            tok["attn_pe_done"] = av_last
            em.wait("vector", av_last)
            em.wait("vector", av_free[p2])
            em.op("vector", lambda e, p2=p2, avA=avA: e.tensor_copy(
                avrawA[p2][:], PS[avA][0:65, :]))
            tok_ev = em.op("vector", lambda e, p2=p2, avB=avB: e.tensor_copy(
                avrawB[p2][:], PS[avB][:]), inc=ve)
            bank_free[avA] = tok_ev
            bank_free[avB] = tok_ev
            tails.append((j, p2, avA, avB, tok_ev))
            flush_tails(False)
        flush_tails(True)

    def phase_D(c):
        em.wait("scalar", tok["sch_ve_done"])
        tok_ev = None
        last = None
        for hs in range(4):
            tiles, tok_ld, slots = load_slab(woT, 512 * hs, wready["wo"])
            for qj in range(4):
                bank = (hs * 4 + qj) % 2
                em.wait("tensor", bank_free[bank])
                em.wait("tensor", tok_ld)
                em.wait("tensor", tok["attn_ready_last"])
                tok_mm = None
                for oc in range(16):
                    def fmm(e, oc=oc, qj=qj, tiles=tiles, bank=bank,
                            first=(oc == 0), lastm=(oc == 15)):
                        return e.matmul(PS[bank][:],
                                        attnfm[oc][:, 128 * qj:128 * (qj + 1)],
                                        tiles[oc][:], start=first, stop=lastm)
                    if oc == 15:
                        tok_mm = em.op("tensor", fmm, inc=pe)
                    else:
                        em.op("tensor", fmm)
                bank_ready[bank] = tok_mm
                last = tok_mm
                em.wait("scalar", tok_mm)
                tok_ev = em.op("scalar", lambda e, qj=qj, hs=hs, bank=bank: e.copy(
                    s_ch[:, 2048 * qj + 512 * hs: 2048 * qj + 512 * (hs + 1)],
                    PS[bank][:]), inc=sc)
                bank_free[bank] = tok_ev
            free_slab(slots, last)
        tok["sch_ready"] = tok_ev

    def phase_E(c):
        tok_g = None
        last_pe = None
        for ns in range(16):
            tiles, tok_ld, slots = load_slab(w0T, 512 * ns, wready["w0"])
            for j4 in range(4):
                nci = 4 * ns + j4
                bank = nci % 2
                em.wait("tensor", bank_free[bank])
                em.wait("tensor", tok_ld)
                if ns == 0 and j4 == 0:
                    em.wait("tensor", tok["xnfm_ready"])
                tok_mm = None
                for hc in range(16):
                    def fmm(e, hc=hc, j4=j4, tiles=tiles, bank=bank,
                            first=(hc == 0), lastm=(hc == 15)):
                        return e.matmul(PS[bank][:], tiles[hc][:, 128 * j4:128 * (j4 + 1)],
                                        xnfm[:, 512 * hc:512 * (hc + 1)],
                                        start=first, stop=lastm)
                    if hc == 15:
                        tok_mm = em.op("tensor", fmm, inc=pe)
                    else:
                        em.op("tensor", fmm)
                bank_ready[bank] = tok_mm
                last_pe = tok_mm
                em.wait("scalar", tok_mm)
                if ns == 0 and j4 == 0:
                    em.wait("scalar", tok["ffn2_pe_done"])
                tok_g = em.op("scalar", lambda e, nci=nci, bank=bank: e.activation(
                    G[nci][:], PS[bank][:], AF.Gelu, bias=b0col[:, nci:nci + 1]), inc=sc)
                bank_free[bank] = tok_g
            free_slab(slots, last_pe)
        tok["xnfm_free"] = last_pe
        tok["gelu_last"] = tok_g

    def phase_F(c):
        seq, cc = c // 4, c % 4
        row0 = cc * CH
        last_pe = None
        tok_cb = None
        for hs in range(4):
            quad = (0, 1, 2, 3) if hs % 2 == 0 else (4, 5, 6, 7)
            for gidx in range(16):
                sl = gidx % 2
                em.wait("sync", w2_free[sl])
                em.wait("sync", wready["w1"])
                src = w1T.ap()[512 * gidx:512 * (gidx + 1),
                               512 * hs:512 * (hs + 1)].rearrange(
                    "(a p) n -> a p n", p=128).transpose([1, 0, 2])
                tok_ld = em.op("sync", lambda e, sl=sl, src=src: e.dma_start(
                    w2slot[sl], src), inc=dS, amt=16)
                last_grp = None
                for a in range(4):
                    nci = 4 * gidx + a
                    for qj in range(4):
                        bank = quad[qj]
                        first = (nci == 0)
                        if first:
                            em.wait("tensor", bank_free[bank])
                            if qj == 0:
                                em.wait("tensor", tok["gelu_last"])
                        if a == 0 and qj == 0:
                            em.wait("tensor", tok_ld)

                        def fmm(e, nci=nci, qj=qj, sl=sl, a=a, bank=bank, first=first):
                            return e.matmul(
                                PS[bank][:], G[nci][:, 128 * qj:128 * (qj + 1)],
                                w2slot[sl][:, 512 * a:512 * (a + 1)],
                                start=first, stop=False)
                        if a == 3 and qj == 3:
                            last_grp = em.op("tensor", fmm, inc=pe)
                        else:
                            em.op("tensor", fmm)
                w2_free[sl] = last_grp
                last_pe = last_grp
            for qj in range(4):
                bank = quad[qj]
                tk = em.op("tensor", lambda e, hs=hs, bank=bank: e.matmul(
                    PS[bank][:], onesr[0:1, 0:128],
                    b1r[0:1, 512 * hs:512 * (hs + 1)], start=False, stop=True), inc=pe)
                bank_ready[bank] = tk
                last_pe = tk
                s2i = (hs * 4 + qj) % 2
                em.wait("scalar", tk)
                em.wait("scalar", gp_free[s2i])
                tok_si = em.op("scalar", lambda e, s2i=s2i, bank=bank: e.activation(
                    gp[s2i][:], PS[bank][:], AF.Silu), inc=sc)
                bank_free[bank] = tok_si
                em.wait("scalar", xp_free[s2i])
                tok_x = em.op("scalar", lambda e, seq=seq, r=row0 + 128 * qj, hs=hs,
                              s2i=s2i: e.dma_start(
                    xp[s2i][:], x_in.ap()[seq, r:r + 128, 512 * hs:512 * (hs + 1)]),
                    inc=dA, amt=16)
                em.wait("vector", tok_si)
                em.wait("vector", tok["sch_ready"])
                em.wait("vector", tp_free[s2i])
                em.op("vector", lambda e, s2i=s2i, qj=qj, hs=hs: e.tensor_tensor(
                    tp[s2i][:], gp[s2i][:],
                    s_ch[:, 2048 * qj + 512 * hs:2048 * qj + 512 * (hs + 1)], ALU.mult))
                em.wait("vector", tok_x)
                tok_cb = em.op("vector", lambda e, s2i=s2i: e.tensor_tensor(
                    tp[s2i][:], tp[s2i][:], xp[s2i][:], ALU.add), inc=ve)
                gp_free[s2i] = tok_cb
                xp_free[s2i] = tok_cb
                em.wait("scalar", tok_cb)
                tok_st = em.op("scalar", lambda e, seq=seq, r=row0 + 128 * qj, hs=hs,
                               s2i=s2i: e.dma_start(
                    y_out.ap()[seq, r:r + 128, 512 * hs:512 * (hs + 1)], tp[s2i][:]),
                    inc=dA, amt=16)
                tp_free[s2i] = tok_st
                tok["out_last"] = tok_st
        tok["ffn2_pe_done"] = last_pe
        tok["sch_ve_done"] = tok_cb

    wready["qkv"] = prol_transpose(wqkv_bf, wqkvT, 2560, 16, tok_c_qkv)
    wready["wo"] = prol_transpose(wo_bf, woT, 2048, 16, tok_c_wo, store_rows=wo_rows)
    phase_A1(0)
    phase_A2(0)
    phase_B(0)
    phase_C(0)
    phase_D(0)
    wready["w0"] = prol_transpose(w0_bf, w0T, NF, 16, tok_c_w0)
    phase_E(0)
    wready["w1"] = prol_transpose(w1_bf, w1T, 2048, 64, tok_c_w1)
    phase_F(0)
    for c in range(1, NCH):
        phase_A1(c)
        phase_A2(c)
        phase_B(c)
        phase_C(c)
        phase_D(c)
        phase_E(c)
        phase_F(c)

    em.wait("sync", tok["out_last"])
    em.wait("sync", (dA, dA.v))
    em.wait("sync", (dS, dS.v))
    em.wait("sync", (dP, dP.v))

    with nc.Block() as block:
        @block.sync
        def _(e):
            for f in em.q["sync"]:
                f(e)

        @block.tensor
        def _(e):
            for f in em.q["tensor"]:
                f(e)

        @block.vector
        def _(e):
            for f in em.q["vector"]:
                f(e)

        @block.scalar
        def _(e):
            for f in em.q["scalar"]:
                f(e)

        @block.gpsimd
        def _(e):
            for f in em.q["gpsimd"]:
                f(e)

    ctx.close()
    return nc


def _tables():
    inv = (1.0 / (10000.0 ** (np.arange(0, D, 2, dtype=np.float32) /
                              np.float32(D)))).astype(np.float32)
    pos = np.arange(S, dtype=np.float32)
    ang = (pos[None, :] * inv[:, None]).astype(np.float32)
    cos32 = np.cos(ang.astype(np.float64)).astype(np.float32)
    sin32 = np.sin(ang.astype(np.float64)).astype(np.float32)
    cosf = np.tile(cos32, (4, 1)).astype(bfdt)
    sinf = np.tile(sin32, (4, 1)).astype(bfdt)
    p = np.zeros((128, 128), np.float32)
    for bb in (0, 64):
        for r in range(32):
            p[bb + r + 32, bb + r] = -1.0
            p[bb + r, bb + r + 32] = 1.0
    trimask = np.triu(np.ones((128, 128), np.float32)).astype(bfdt)
    idf = np.eye(64, dtype=np.float32)
    idb = np.eye(128, dtype=np.float32).astype(bfdt)
    return cosf, sinf, p.astype(bfdt), trimask, idf, idb


def kernel(L0_attn_hidden_states, L0_attn_w_qkv, L0_attn_w_o,
           L0_ffn_W0, L0_ffn_bias0, L0_ffn_W1, L0_ffn_bias1):
    x = np.ascontiguousarray(np.asarray(L0_attn_hidden_states, dtype=np.float32))
    cosf, sinf, p128t, trimask, idf, idb = _tables()
    shared = dict(
        wqkv=np.ascontiguousarray(np.asarray(L0_attn_w_qkv, np.float32)),
        wo=np.ascontiguousarray(np.asarray(L0_attn_w_o, np.float32)),
        w0=np.ascontiguousarray(np.asarray(L0_ffn_W0, np.float32)),
        b0=np.ascontiguousarray(np.asarray(L0_ffn_bias0, np.float32)),
        w1=np.ascontiguousarray(np.asarray(L0_ffn_W1, np.float32)),
        b1=np.ascontiguousarray(np.asarray(L0_ffn_bias1, np.float32)),
        tcos=cosf, tsin=sinf, tp128=p128t, ttrim=trimask, tidf=idf, tidb=idb,
    )
    nc = build_program()
    in_maps = []
    for i in range(NCORES):
        m = dict(shared)
        m["x"] = np.ascontiguousarray(x[BLOC * i:BLOC * (i + 1)])
        in_maps.append(m)
    res = run_bass_kernel_spmd(nc, in_maps, core_ids=list(range(NCORES)))
    return np.concatenate([np.asarray(r["y"]) for r in res.results], axis=0)


# version 11
# speedup vs baseline: 1.0763x; 1.0763x over previous
import numpy as np
import ml_dtypes
from contextlib import ExitStack
import concourse.bass as bass
import concourse.mybir as mybir
from concourse.bass_utils import run_bass_kernel_spmd

F32 = mybir.dt.float32
BF16 = mybir.dt.bfloat16
U8 = mybir.dt.uint8
AF = mybir.ActivationFunctionType
ALU = mybir.AluOpType
AX = mybir.AxisListType
bfdt = ml_dtypes.bfloat16

NCORES = 8
B, S, H = 16, 2048, 2048
BLOC = B // NCORES
CH = 512
NCH = BLOC * S // CH
HQ, HKV, D = 32, 4, 64
NF = 8192
KB = 1024
VBLK = 386
INV2048 = 1.0 / 2048.0


class Sem:
    def __init__(self, h):
        self.h = h
        self.v = 0

    def inc(self, amt=1):
        self.v += amt
        return self.v


class Em:
    ENGS = ("tensor", "vector", "scalar", "sync", "gpsimd")

    def __init__(self):
        self.q = {e: [] for e in self.ENGS}

    def wait(self, eng, tok):
        if tok is None:
            return
        sem, val = tok
        if val <= 0:
            return
        assert val <= sem.v, f"wait on future token {val} > {sem.v}"
        self.q[eng].append(lambda e, h=sem.h, v=val: e.wait_ge(h, v))

    def op(self, eng, fn, inc=None, amt=1):
        if inc is None:
            self.q[eng].append(fn)
            return None
        val = inc.inc(amt)

        def run(e, f=fn, h=inc.h, a=amt):
            f(e).then_inc(h, a)
        self.q[eng].append(run)
        return (inc, val)


def build_program():
    nc = bass.Bass("TRN2", target_bir_lowering=False)

    x_in = nc.dram_tensor("x", [BLOC, S, H], F32, kind="ExternalInput")
    wqkv = nc.dram_tensor("wqkv", [2560, H], F32, kind="ExternalInput")
    wo = nc.dram_tensor("wo", [H, H], F32, kind="ExternalInput")
    w0 = nc.dram_tensor("w0", [NF, H], F32, kind="ExternalInput")
    b0 = nc.dram_tensor("b0", [NF], F32, kind="ExternalInput")
    w1 = nc.dram_tensor("w1", [H, NF], F32, kind="ExternalInput")
    b1 = nc.dram_tensor("b1", [H], F32, kind="ExternalInput")
    t_cos = nc.dram_tensor("tcos", [128, S], BF16, kind="ExternalInput")
    t_sin = nc.dram_tensor("tsin", [128, S], BF16, kind="ExternalInput")
    t_p128 = nc.dram_tensor("tp128", [128, 128], BF16, kind="ExternalInput")
    t_trim = nc.dram_tensor("ttrim", [128, 128], BF16, kind="ExternalInput")
    t_idf = nc.dram_tensor("tidf", [64, 64], F32, kind="ExternalInput")
    t_idb = nc.dram_tensor("tidb", [128, 128], BF16, kind="ExternalInput")
    y_out = nc.dram_tensor("y", [BLOC, S, H], F32, kind="ExternalOutput")

    wqkv_bf = nc.dram_tensor("wqkv_bf", [2560, H], BF16, kind="Internal")
    wo_bf = nc.dram_tensor("wo_bf", [H, H], BF16, kind="Internal")
    w0_bf = nc.dram_tensor("w0_bf", [NF, H], BF16, kind="Internal")
    w1_bf = nc.dram_tensor("w1_bf", [H, NF], BF16, kind="Internal")
    wqkvT = nc.dram_tensor("wqkvT", [H, 2560], BF16, kind="Internal")
    woT = nc.dram_tensor("woT", [H, H], BF16, kind="Internal")
    w0T = nc.dram_tensor("w0T", [H, NF], BF16, kind="Internal")
    w1T = nc.dram_tensor("w1T", [NF, H], BF16, kind="Internal")
    xn_dram = nc.dram_tensor("xn_dram", [2, CH, H], BF16, kind="Internal")

    ctx = ExitStack()
    sb = lambda name, shape, dt: ctx.enter_context(nc.sbuf_tensor(name, list(shape), dt))

    xnfm = sb("xnfm", [128, 16 * 512], BF16)
    s_ch = sb("s_ch", [128, 4 * 2048], BF16)
    kcache = sb("kcache", [128, 2 * 2048], BF16)
    vcache = sb("vcache", [128, 16 * VBLK], BF16)
    wslab = sb("wslab", [128, 4 * 4096], BF16)
    w2r = sb("w2r", [128, 2 * 2048], BF16)
    cosf = sb("cosf", [128, S], BF16)
    sinf = sb("sinf", [128, S], BF16)
    p128 = sb("p128", [128, 128], BF16)
    trim = sb("trim", [128, 128], BF16)
    idf = sb("idf", [64, 64], F32)
    idb = sb("idb", [128, 128], BF16)
    b0sb = sb("b0sb", [64, 128], F32)
    b0col = sb("b0col", [128, 64], F32)
    b1r = sb("b1r", [1, H], BF16)
    onesr = sb("onesr", [1, 512], BF16)
    kbst = sb("kbst", [128, 4], F32)
    lnst = sb("lnst", [128, 24], F32)

    arena_items = []
    ARENA = 96 * KB

    def place(name, shape, dt, off):
        nbytes = int(np.prod(shape[1:])) * mybir.dt.size(dt)
        assert off + nbytes <= ARENA, f"arena overflow {name}"
        h = nc.virtual_sbuf_tensor(name, list(shape), dt)
        arena_items.append((h, off))
        return h

    Qfm = [place(f"qfm{i}", [128, 512], BF16, i * KB) for i in range(16)]
    attnfm = [place(f"afm{i}", [128, 512], BF16, 16 * KB + i * KB) for i in range(16)]
    x_t = [place(f"xt{i}", [128, 2048], F32, 32 * KB + i * 8 * KB) for i in range(2)]
    sqs = place("sqs", [128, 2048], F32, 48 * KB)
    xn_tm = [place(f"xntm{i}", [128, 2048], BF16, 56 * KB + i * 4 * KB) for i in range(2)]
    G = [place(f"g{i}", [128, 512], BF16, i * KB) for i in range(64)]
    Qs = [place(f"qs{i}", [128, 512], BF16, 64 * KB + i * KB) for i in range(2)]
    t1b = [place(f"t1b{i}", [128, 512], F32, 66 * KB + i * 2 * KB) for i in range(2)]
    probsT = [[place(f"pt{h}_{i}", [128, 512], BF16, 64 * KB + (2 * h + i) * KB)
               for i in range(2)] for h in range(2)]
    avrawA = [place(f"avA{i}", [65, 512], BF16, 70 * KB + i * KB) for i in range(2)]
    avrawB = [place(f"avB{i}", [128, 512], BF16, 72 * KB + i * KB) for i in range(2)]
    recfA = [place(f"rfA{i}", [1, 512], F32, 74 * KB + i * 2 * KB) for i in range(2)]
    recfB = [place(f"rfB{i}", [1, 512], F32, 78 * KB + i * 2 * KB) for i in range(2)]
    recbA = [place(f"rbA{i}", [1, 512], BF16, 82 * KB + i * KB) for i in range(2)]
    recbB = [place(f"rbB{i}", [1, 512], BF16, 84 * KB + i * KB) for i in range(2)]
    gp = [place(f"gp{i}", [128, 512], BF16, 86 * KB + i * KB) for i in range(2)]
    xp = [place(f"xp{i}", [128, 512], F32, 88 * KB + i * 2 * KB) for i in range(2)]
    tp = [place(f"tp{i}", [128, 512], F32, 92 * KB + i * 2 * KB) for i in range(2)]

    arena = ctx.enter_context(nc.sbuf_tensor("arena", [128, ARENA], U8))
    base = nc.lookup_mloc(arena).addr
    for h, off in arena_items:
        nc._tensor(h.name, h.shape, h.dtype, type="SB")
        m = nc.lookup_mloc(h)
        m.allocated = True
        m.addr = base + off

    PS = [ctx.enter_context(nc.psum_tensor(f"ps{i}", [128, 512], F32)) for i in range(8)]

    dS = Sem(ctx.enter_context(nc.semaphore("dS")))
    dA = Sem(ctx.enter_context(nc.semaphore("dA")))
    dP = Sem(ctx.enter_context(nc.semaphore("dP")))
    pe = Sem(ctx.enter_context(nc.semaphore("pe")))
    ve = Sem(ctx.enter_context(nc.semaphore("ve")))
    sc = Sem(ctx.enter_context(nc.semaphore("sc")))

    em = Em()

    em.op("vector", lambda e: e.memset(kbst[:, 0:1], -50.0))
    em.op("vector", lambda e: e.memset(kbst[:, 1:2], 1e-5))
    em.op("vector", lambda e: e.memset(onesr[:], 1.0))
    em.op("vector", lambda e: e.memset(vcache[:], 0.0))
    vc4 = vcache[:].rearrange("p (t h x) -> p t h x", t=16, h=2, x=193)
    em.op("vector", lambda e: e.memset(vc4[:, :, :, 64:65], 1.0))
    tok_vinit = em.op("vector", lambda e: e.memset(vc4[:, :, :, 65:66], 1.0), inc=ve)

    em.op("sync", lambda e: e.dma_start(cosf[:], t_cos.ap()), inc=dS, amt=16)
    em.op("sync", lambda e: e.dma_start(sinf[:], t_sin.ap()), inc=dS, amt=16)
    em.op("sync", lambda e: e.dma_start(p128[:], t_p128.ap()), inc=dS, amt=16)
    em.op("sync", lambda e: e.dma_start(trim[:], t_trim.ap()), inc=dS, amt=16)
    em.op("sync", lambda e: e.dma_start(idf[:], t_idf.ap()), inc=dS, amt=16)
    em.op("sync", lambda e: e.dma_start(idb[:], t_idb.ap()), inc=dS, amt=16)
    tok_tabs = em.op("sync", lambda e: e.dma_start(
        b0sb[:], b0.ap().rearrange("(a p) -> a p", p=128)), inc=dS, amt=16)

    em.wait("tensor", tok_tabs)
    tok_b0t = em.op("tensor", lambda e: e.transpose(PS[0][0:128, 0:64], b0sb[:], idf[:]), inc=pe)
    em.wait("vector", tok_b0t)
    tok_b0c = em.op("vector", lambda e: e.tensor_copy(b0col[:], PS[0][:, 0:64]), inc=ve)

    for g2 in range(2):
        for half in range(2):
            r0 = 1024 * g2 + 512 * half
            qsrc = wqkv.ap()[r0:r0 + 512, :].rearrange(
                "(p8 d) c -> p8 d c", p8=8, d=64)
            qdst = wqkv_bf.ap()[1024 * g2:1024 * (g2 + 1), :].rearrange(
                "(p8 half d) c -> half p8 d c", p8=8, half=2, d=64)[half]
            em.op("gpsimd", lambda e, qdst=qdst, qsrc=qsrc: e.dma_start(qdst, qsrc),
                  inc=dP, amt=16)
    tok_c_qkv = em.op("gpsimd", lambda e: e.dma_start(
        wqkv_bf.ap()[2048:2560, :], wqkv.ap()[2048:2560, :]), inc=dP, amt=16)
    tok_c_wo = em.op("gpsimd", lambda e: e.dma_start(wo_bf.ap(), wo.ap()), inc=dP, amt=16)
    tok_c_w0 = None
    for i in range(4):
        tok_c_w0 = em.op("gpsimd", lambda e, i=i: e.dma_start(
            w0_bf.ap()[2048 * i:2048 * (i + 1), :],
            w0.ap()[2048 * i:2048 * (i + 1), :]), inc=dP, amt=16)
    tok_c_w1 = None
    for i in range(4):
        tok_c_w1 = em.op("gpsimd", lambda e, i=i: e.dma_start(
            w1_bf.ap()[512 * i:512 * (i + 1), :],
            w1.ap()[512 * i:512 * (i + 1), :]), inc=dP, amt=16)
    em.op("gpsimd", lambda e: e.dma_start(b1r[:], b1.ap().unsqueeze(0)), inc=dP, amt=16)

    slab_state = [dict(free=None) for _ in range(4)]
    slab_rr = [0]

    def get_pair():
        s0 = slab_rr[0]
        if s0 + 2 > 4:
            s0 = 0
        slab_rr[0] = (s0 + 2) % 4
        em.wait("sync", slab_state[s0]["free"])
        em.wait("sync", slab_state[s0 + 1]["free"])
        return s0, wslab[:, 4096 * s0: 4096 * s0 + 8192]

    def prol_transpose(src_bf, dst_T, R, blocks, cast_tok, store_rows=None):
        last = None
        pending = []

        def do_store(i, s0, view, tok_ld):
            nonlocal last
            em.wait("sync", tok_ld)
            if store_rows is None:
                tok_st = em.op("sync", lambda e, i=i, view=view: e.dma_start(
                    dst_T.ap()[128 * i:128 * (i + 1), :], view[:, 0:R]),
                    inc=dS, amt=16)
            else:
                for (plo, phi, drow) in store_rows(i):
                    tok_st = em.op("sync", lambda e, view=view, plo=plo, phi=phi,
                                   drow=drow: e.dma_start(
                        dst_T.ap()[drow:drow + (phi - plo), :],
                        view[plo:phi, 0:R]), inc=dS, amt=16)
            slab_state[s0]["free"] = tok_st
            slab_state[s0 + 1]["free"] = tok_st
            last = tok_st

        for i in range(blocks):
            s0, view = get_pair()
            em.wait("sync", cast_tok)
            tok_ld = em.op("sync", lambda e, i=i, view=view: e.dma_start(
                view[:, 0:R], src_bf.ap()[:, 128 * i:128 * (i + 1)],
                transpose=True), inc=dS, amt=16)
            pending.append((i, s0, view, tok_ld))
            if len(pending) == 2:
                do_store(*pending.pop(0))
        while pending:
            do_store(*pending.pop(0))
        return last

    def wo_rows(i):
        out = []
        for k, hh in enumerate((2 * i, 2 * i + 1)):
            p = hh % 8 + 8 * (hh // 16)
            half = (hh // 8) % 2
            out.append((64 * k, 64 * (k + 1), p * 128 + half * 64))
        return out

    wready = {}

    w2slot = [w2r[:, 2048 * i:2048 * (i + 1)] for i in range(2)]
    w2_free = [None, None]

    bank_ready = [tok_b0t] + [None] * 7
    bank_free = [tok_b0c] + [None] * 7

    tok = {
        "xnfm_ready": None, "xnfm_free": None,
        "attn_pe_done": None, "ffn2_pe_done": None,
        "sch_ve_done": None, "sch_ready": None,
        "gelu_last": None, "rope_done": None, "v_done": None,
        "attn_ready_last": None, "out_last": None, "xnd_stored": None,
    }
    xt_free = [None, None]
    xntm_free = [None, None]
    xnd_free = [None, None]
    qs_free = [None, None]
    t1_free = [None, None]
    pt_free = [[None, None], [None, None]]
    av_free = [None, None]
    rec_free = [None, None]
    gp_free = [None, None]
    xp_free = [None, None]
    tp_free = [None, None]

    def load_slab(src_T, col0, wtok):
        s0, view = get_pair()
        em.wait("sync", wtok)
        src = src_T.ap()[:, col0:col0 + 512].rearrange(
            "(hc p) n -> hc p n", p=128).transpose([1, 0, 2])
        tok_ld = em.op("sync", lambda e, view=view, src=src: e.dma_start(view, src),
                       inc=dS, amt=16)
        tiles = [view[:, 512 * i: 512 * (i + 1)] for i in range(16)]
        return tiles, tok_ld, (s0, s0 + 1)

    def free_slab(slots, tk):
        for s in slots:
            slab_state[s]["free"] = tk

    def phase_A1(c):
        seq, cc = c // 4, c % 4
        row0 = cc * CH
        em.wait("scalar", tok["ffn2_pe_done"])
        em.wait("vector", tok["ffn2_pe_done"])
        last_store = None
        for half in range(2):
            tiles = (2 * half, 2 * half + 1)
            ve1 = {}
            for t in tiles:
                sl = t % 2
                em.wait("scalar", xt_free[sl])
                tok_x = em.op("scalar", lambda e, seq=seq, r=row0 + 128 * t, sl=sl:
                              e.dma_start(x_t[sl][:], x_in.ap()[seq, r:r + 128, :]),
                              inc=dA, amt=16)
                em.wait("vector", tok_x)
                em.op("vector", lambda e, sl=sl: e.tensor_tensor(
                    sqs[:], x_t[sl][:], x_t[sl][:], ALU.mult))
                em.op("vector", lambda e, t=t: e.reduce_sum(
                    lnst[:, 6 * t + 1:6 * t + 2], sqs[:], AX.X))
                ve1[t] = em.op("vector", lambda e, t=t, sl=sl: e.reduce_sum(
                    lnst[:, 6 * t + 0:6 * t + 1], x_t[sl][:], AX.X), inc=ve)
            sc1 = {}
            for t in tiles:
                em.wait("scalar", ve1[t])
                sc1[t] = em.op("scalar", lambda e, t=t: e.activation(
                    lnst[:, 6 * t + 2:6 * t + 3], lnst[:, 6 * t + 0:6 * t + 1],
                    AF.Square, scale=INV2048), inc=sc)
            ve2 = {}
            for t in tiles:
                em.wait("vector", sc1[t])
                ve2[t] = em.op("vector", lambda e, t=t: e.tensor_scalar(
                    lnst[:, 6 * t + 3:6 * t + 4], lnst[:, 6 * t + 1:6 * t + 2],
                    INV2048, lnst[:, 6 * t + 2:6 * t + 3], ALU.mult, ALU.subtract),
                    inc=ve)
            sc2 = {}
            for t in tiles:
                em.wait("scalar", ve2[t])
                em.op("scalar", lambda e, t=t: e.activation(
                    lnst[:, 6 * t + 4:6 * t + 5], lnst[:, 6 * t + 3:6 * t + 4],
                    AF.Ln, bias=kbst[:, 1:2]))
            for t in tiles:
                sc2[t] = em.op("scalar", lambda e, t=t: e.activation(
                    lnst[:, 6 * t + 4:6 * t + 5], lnst[:, 6 * t + 4:6 * t + 5],
                    AF.Exp, scale=-0.5), inc=sc)
            ve3 = {}
            for t in tiles:
                em.wait("vector", sc2[t])
                ve3[t] = em.op("vector", lambda e, t=t: e.tensor_scalar(
                    lnst[:, 6 * t + 5:6 * t + 6], lnst[:, 6 * t + 0:6 * t + 1],
                    lnst[:, 6 * t + 4:6 * t + 5], -INV2048, ALU.mult, ALU.mult), inc=ve)
            for t in tiles:
                sl = t % 2
                em.wait("scalar", ve3[t])
                em.wait("scalar", xntm_free[sl])
                tk = em.op("scalar", lambda e, t=t, sl=sl: e.activation(
                    xn_tm[sl][:], x_t[sl][:], AF.Identity,
                    bias=lnst[:, 6 * t + 5:6 * t + 6],
                    scale=lnst[:, 6 * t + 4:6 * t + 5]), inc=sc)
                xt_free[sl] = tk
                em.wait("tensor", tk)
                if t == 0:
                    em.wait("scalar", tok["xnfm_free"])
                tr_tok = None
                for grp in range(4):
                    rb = 2 + (grp % 2)
                    pbf = PS[rb][:].bitcast(BF16)
                    em.wait("tensor", bank_free[rb])
                    for i in range(4):
                        hc = 4 * grp + i
                        tr_tok = em.op("tensor", lambda e, sl=sl, hc=hc, i=i, pbf=pbf:
                                       e.transpose(
                            pbf[:, 128 * i:128 * (i + 1)],
                            xn_tm[sl][:, 128 * hc:128 * (hc + 1)], idb[:]),
                            inc=pe if i == 3 else None)
                    bank_ready[rb] = tr_tok
                    em.wait("scalar", tr_tok)
                    dst = xnfm[:].rearrange("p (hc n) -> p hc n", hc=16)[
                        :, 4 * grp:4 * grp + 4, 128 * t:128 * (t + 1)]
                    cp_tok = em.op("scalar", lambda e, dst=dst, pbf=pbf: e.copy(
                        dst, pbf[:, 0:512].rearrange("p (a n) -> p a n", a=4)), inc=sc)
                    bank_free[rb] = cp_tok
                xntm_free[sl] = tr_tok
                tok["xnfm_ready"] = cp_tok

    def phase_A2(c):
        pass

    def phase_B(c):
        seq, cc = c // 4, c % 4
        apd = tok["attn_pe_done"]
        em.wait("vector", apd)
        em.wait("vector", tok["ffn2_pe_done"])
        em.wait("scalar", apd)
        em.wait("vector", tok_vinit)
        csl = cosf[:, CH * cc: CH * cc + 512]
        ssl = sinf[:, CH * cc: CH * cc + 512]
        tok_v = None
        pend_rope = None

        def flush_rope(oc, sp, tok_qs):
            rbank = 2 + (oc % 2)
            em.wait("tensor", bank_free[rbank])
            em.wait("tensor", tok_qs)
            tok_rot = em.op("tensor", lambda e, sp=sp, rbank=rbank: e.matmul(
                PS[rbank][:], p128[:], Qs[sp][:], start=True, stop=True), inc=pe)
            bank_ready[rbank] = tok_rot
            if oc < 16:
                dest = Qfm[oc][:]
            else:
                kk2 = oc - 16
                dest = kcache[:, 2048 * kk2 + CH * cc: 2048 * kk2 + CH * cc + 512]
            em.wait("vector", tok_qs)
            em.wait("vector", t1_free[sp])
            em.op("vector", lambda e, sp=sp, csl=csl: e.tensor_tensor(
                t1b[sp][:], Qs[sp][:], csl, ALU.mult))
            em.wait("vector", tok_rot)
            em.op("vector", lambda e, sp=sp, ssl=ssl, rbank=rbank: e.tensor_tensor(
                Qs[sp][:], PS[rbank][:], ssl, ALU.mult))
            tk3 = em.op("vector", lambda e, sp=sp, dest=dest: e.tensor_tensor(
                dest, t1b[sp][:], Qs[sp][:], ALU.add), inc=ve)
            qs_free[sp] = tk3
            t1_free[sp] = tk3
            bank_free[rbank] = tk3
            if oc == 17:
                tok["rope_done"] = tk3
        for osl in range(5):
            tiles, tok_ld, slots = load_slab(wqkvT, 512 * osl, wready["qkv"])
            last_use = None
            for oc in range(4 * osl, min(4 * osl + 4, 18)):
                bank = oc % 2
                j = oc - 4 * osl
                em.wait("tensor", bank_free[bank])
                em.wait("tensor", tok_ld)
                em.wait("tensor", tok["xnfm_ready"])
                tok_mm = None
                for hc in range(16):
                    def fmm(e, hc=hc, j=j, tiles=tiles, bank=bank,
                            first=(hc == 0), lastm=(hc == 15)):
                        return e.matmul(PS[bank][:], tiles[hc][:, 128 * j:128 * (j + 1)],
                                        xnfm[:, 512 * hc:512 * (hc + 1)],
                                        start=first, stop=lastm)
                    if hc == 15:
                        tok_mm = em.op("tensor", fmm, inc=pe)
                    else:
                        em.op("tensor", fmm)
                bank_ready[bank] = tok_mm
                last_use = tok_mm
                sp = oc % 2
                em.wait("scalar", tok_mm)
                em.wait("scalar", qs_free[sp])
                tok_qs = em.op("scalar", lambda e, sp=sp, bank=bank: e.copy(
                    Qs[sp][:], PS[bank][:]), inc=sc)
                bank_free[bank] = tok_qs
                if pend_rope is not None:
                    flush_rope(*pend_rope)
                pend_rope = (oc, sp, tok_qs)
            if osl == 4:
                voff = [0, 65 + 64, 193, 258 + 64]
                for qt in range(4):
                    bank = qt % 2
                    em.wait("tensor", bank_free[bank])
                    tok_mm = None
                    for hc in range(16):
                        def fmm(e, hc=hc, qt=qt, tiles=tiles, bank=bank,
                                first=(hc == 0), lastm=(hc == 15)):
                            return e.matmul(
                                PS[bank][:, 0:256],
                                xnfm[:, 512 * hc + 128 * qt:512 * hc + 128 * (qt + 1)],
                                tiles[hc][:, 256:512], start=first, stop=lastm)
                        if hc == 15:
                            tok_mm = em.op("tensor", fmm, inc=pe)
                        else:
                            em.op("tensor", fmm)
                    bank_ready[bank] = tok_mm
                    last_use = tok_mm
                    em.wait("scalar", tok_mm)
                    for g in range(4):
                        tok_v = em.op("scalar", lambda e, g=g, bank=bank, kt=4 * cc + qt,
                                      voff=voff: e.copy(
                            vcache[:, VBLK * kt + voff[g]: VBLK * kt + voff[g] + 64],
                            PS[bank][:, 64 * g:64 * (g + 1)]), inc=sc)
                    bank_free[bank] = tok_v
            free_slab(slots, last_use)
        if pend_rope is not None:
            flush_rope(*pend_rope)
            pend_rope = None
        tok["v_done"] = tok_v

    def phase_C(c):
        seq, cc = c // 4, c % 4
        nkt = 4 * cc + 4
        tails = []

        def flush_tails(force):
            while tails:
                j, p2, avA, avB, tok_ev = tails.pop(0)
                em.wait("vector", tok_ev)
                em.wait("vector", rec_free[p2])
                em.op("vector", lambda e, p2=p2: e.reciprocal(
                    recfA[p2][:], avrawA[p2][64:65, :]))
                em.op("vector", lambda e, p2=p2: e.reciprocal(
                    recfB[p2][:], avrawB[p2][0:1, :]))
                em.op("vector", lambda e, p2=p2: e.tensor_copy(recbA[p2][:], recfA[p2][:]))
                tok_rb = em.op("vector", lambda e, p2=p2: e.tensor_copy(
                    recbB[p2][:], recfB[p2][:]), inc=ve)
                em.wait("tensor", tok_rb)
                em.wait("tensor", bank_free[avA])
                em.wait("tensor", bank_free[avB])
                em.op("tensor", lambda e, p2=p2, avA=avA: e.matmul(
                    PS[avA][:], onesr[0:1, 0:128], recbA[p2][:], start=True, stop=True))
                tok_bc = em.op("tensor", lambda e, p2=p2, avB=avB: e.matmul(
                    PS[avB][:], onesr[0:1, 0:128], recbB[p2][:], start=True, stop=True),
                    inc=pe)
                bank_ready[avA] = tok_bc
                bank_ready[avB] = tok_bc
                tok["attn_pe_done"] = tok_bc
                em.wait("vector", tok_bc)
                em.op("vector", lambda e, j=j, p2=p2, avA=avA: e.tensor_tensor(
                    attnfm[j][0:64, :], avrawA[p2][0:64, :], PS[avA][0:64, :], ALU.mult))
                tok_nm = em.op("vector", lambda e, j=j, p2=p2, avB=avB: e.tensor_tensor(
                    attnfm[j][64:128, :], avrawB[p2][64:128, :], PS[avB][64:128, :],
                    ALU.mult), inc=ve)
                bank_free[avA] = tok_nm
                bank_free[avB] = tok_nm
                av_free[p2] = tok_nm
                rec_free[p2] = tok_nm
                tok["attn_ready_last"] = tok_nm

        for j in range(16):
            kk = j // 8
            p2 = j % 2
            avA, avB = 4 + 2 * p2, 5 + 2 * p2
            av_last = None
            pend = None

            def emit_av(kt, col0, tok_e, mask_tok, lastm, kk=kk, avA=avA, avB=avB):
                nonlocal av_last
                em.wait("tensor", tok_e)
                if mask_tok is not None:
                    em.wait("tensor", mask_tok)
                offA = VBLK * kt + (0 if kk == 0 else 193)
                offB = VBLK * kt + (65 if kk == 0 else 258)
                em.op("tensor", lambda e, kt=kt, col0=col0, offA=offA, avA=avA,
                      first=(kt == 0), lastm=lastm: e.matmul(
                    PS[avA][0:65, col0:512], vcache[:, offA:offA + 65],
                    probsT[0][kt % 2][:, col0:512], start=first, stop=lastm))
                tok_av = em.op("tensor", lambda e, kt=kt, col0=col0, offB=offB, avB=avB,
                               first=(kt == 0), lastm=lastm: e.matmul(
                    PS[avB][:, col0:512], vcache[:, offB:offB + 128],
                    probsT[1][kt % 2][:, col0:512], start=first, stop=lastm), inc=pe)
                pt_free[0][kt % 2] = tok_av
                pt_free[1][kt % 2] = tok_av
                av_last = tok_av

            for kt in range(nkt):
                dr = kt - 4 * cc
                col0 = max(dr, 0) * 128
                sbA, sbB = kt % 2, 2 + kt % 2
                em.wait("tensor", bank_free[sbA])
                em.wait("tensor", bank_free[sbB])
                if kt == 0:
                    if j == 0:
                        em.wait("tensor", tok["rope_done"])
                        em.wait("tensor", tok["v_done"])
                    em.wait("tensor", av_free[p2])
                    em.wait("tensor", bank_free[avA])
                    em.wait("tensor", bank_free[avB])
                em.op("tensor", lambda e, sbA=sbA, kk=kk, kt=kt, j=j, col0=col0: e.matmul(
                    PS[sbA][:, col0:512],
                    kcache[0:64, 2048 * kk + 128 * kt: 2048 * kk + 128 * (kt + 1)],
                    Qfm[j][0:64, col0:512], start=True, stop=True))
                tok_s = em.op("tensor", lambda e, sbB=sbB, kk=kk, kt=kt, j=j, col0=col0:
                              e.matmul(
                    PS[sbB][:, col0:512],
                    kcache[64:128, 2048 * kk + 128 * kt: 2048 * kk + 128 * (kt + 1)],
                    Qfm[j][64:128, col0:512], start=True, stop=True), inc=pe)
                bank_ready[sbA] = tok_s
                bank_ready[sbB] = tok_s
                em.wait("scalar", tok_s)
                em.wait("scalar", pt_free[0][kt % 2])
                em.wait("scalar", pt_free[1][kt % 2])
                em.op("scalar", lambda e, kt=kt, col0=col0, sbA=sbA: e.activation(
                    probsT[0][kt % 2][:, col0:512], PS[sbA][:, col0:512], AF.Exp,
                    bias=kbst[:, 0:1], scale=0.125))
                tok_e = em.op("scalar", lambda e, kt=kt, col0=col0, sbB=sbB: e.activation(
                    probsT[1][kt % 2][:, col0:512], PS[sbB][:, col0:512], AF.Exp,
                    bias=kbst[:, 0:1], scale=0.125), inc=sc)
                bank_free[sbA] = tok_e
                bank_free[sbB] = tok_e
                mask_tok = None
                if dr >= 0:
                    em.wait("vector", tok_e)
                    em.op("vector", lambda e, kt=kt, col0=col0: e.tensor_tensor(
                        probsT[0][kt % 2][:, col0:col0 + 128],
                        probsT[0][kt % 2][:, col0:col0 + 128], trim[:], ALU.mult))
                    mask_tok = em.op("vector", lambda e, kt=kt, col0=col0: e.tensor_tensor(
                        probsT[1][kt % 2][:, col0:col0 + 128],
                        probsT[1][kt % 2][:, col0:col0 + 128], trim[:], ALU.mult), inc=ve)
                if pend is not None:
                    emit_av(*pend, lastm=False)
                pend = (kt, col0, tok_e, mask_tok)
                if kt == 1:
                    flush_tails(False)
            emit_av(*pend, lastm=True)
            pend = None
            bank_ready[avA] = av_last
            bank_ready[avB] = av_last
            tok["attn_pe_done"] = av_last
            em.wait("vector", av_last)
            em.wait("vector", av_free[p2])
            em.op("vector", lambda e, p2=p2, avA=avA: e.tensor_copy(
                avrawA[p2][:], PS[avA][0:65, :]))
            tok_ev = em.op("vector", lambda e, p2=p2, avB=avB: e.tensor_copy(
                avrawB[p2][:], PS[avB][:]), inc=ve)
            bank_free[avA] = tok_ev
            bank_free[avB] = tok_ev
            tails.append((j, p2, avA, avB, tok_ev))
        flush_tails(True)

    def phase_D(c):
        em.wait("scalar", tok["sch_ve_done"])
        tok_ev = None
        last = None
        for hs in range(4):
            tiles, tok_ld, slots = load_slab(woT, 512 * hs, wready["wo"])
            for qj in range(4):
                bank = (hs * 4 + qj) % 2
                em.wait("tensor", bank_free[bank])
                em.wait("tensor", tok_ld)
                em.wait("tensor", tok["attn_ready_last"])
                tok_mm = None
                for oc in range(16):
                    def fmm(e, oc=oc, qj=qj, tiles=tiles, bank=bank,
                            first=(oc == 0), lastm=(oc == 15)):
                        return e.matmul(PS[bank][:],
                                        attnfm[oc][:, 128 * qj:128 * (qj + 1)],
                                        tiles[oc][:], start=first, stop=lastm)
                    if oc == 15:
                        tok_mm = em.op("tensor", fmm, inc=pe)
                    else:
                        em.op("tensor", fmm)
                bank_ready[bank] = tok_mm
                last = tok_mm
                em.wait("scalar", tok_mm)
                tok_ev = em.op("scalar", lambda e, qj=qj, hs=hs, bank=bank: e.copy(
                    s_ch[:, 2048 * qj + 512 * hs: 2048 * qj + 512 * (hs + 1)],
                    PS[bank][:]), inc=sc)
                bank_free[bank] = tok_ev
            free_slab(slots, last)
        tok["sch_ready"] = tok_ev

    def phase_E(c):
        tok_g = None
        last_pe = None
        for ns in range(16):
            tiles, tok_ld, slots = load_slab(w0T, 512 * ns, wready["w0"])
            for j4 in range(4):
                nci = 4 * ns + j4
                bank = nci % 2
                em.wait("tensor", bank_free[bank])
                em.wait("tensor", tok_ld)
                if ns == 0 and j4 == 0:
                    em.wait("tensor", tok["xnfm_ready"])
                tok_mm = None
                for hc in range(16):
                    def fmm(e, hc=hc, j4=j4, tiles=tiles, bank=bank,
                            first=(hc == 0), lastm=(hc == 15)):
                        return e.matmul(PS[bank][:], tiles[hc][:, 128 * j4:128 * (j4 + 1)],
                                        xnfm[:, 512 * hc:512 * (hc + 1)],
                                        start=first, stop=lastm)
                    if hc == 15:
                        tok_mm = em.op("tensor", fmm, inc=pe)
                    else:
                        em.op("tensor", fmm)
                bank_ready[bank] = tok_mm
                last_pe = tok_mm
                em.wait("scalar", tok_mm)
                if ns == 0 and j4 == 0:
                    em.wait("scalar", tok["ffn2_pe_done"])
                tok_g = em.op("scalar", lambda e, nci=nci, bank=bank: e.activation(
                    G[nci][:], PS[bank][:], AF.Gelu, bias=b0col[:, nci:nci + 1]), inc=sc)
                bank_free[bank] = tok_g
            free_slab(slots, last_pe)
        tok["xnfm_free"] = last_pe
        tok["gelu_last"] = tok_g

    def phase_F(c):
        seq, cc = c // 4, c % 4
        row0 = cc * CH
        last_pe = None
        tok_cb = None
        for hs in range(4):
            quad = (0, 1, 2, 3) if hs % 2 == 0 else (4, 5, 6, 7)
            for gidx in range(16):
                sl = gidx % 2
                em.wait("sync", w2_free[sl])
                em.wait("sync", wready["w1"])
                src = w1T.ap()[512 * gidx:512 * (gidx + 1),
                               512 * hs:512 * (hs + 1)].rearrange(
                    "(a p) n -> a p n", p=128).transpose([1, 0, 2])
                tok_ld = em.op("sync", lambda e, sl=sl, src=src: e.dma_start(
                    w2slot[sl], src), inc=dS, amt=16)
                last_grp = None
                for a in range(4):
                    nci = 4 * gidx + a
                    for qj in range(4):
                        bank = quad[qj]
                        first = (nci == 0)
                        if first:
                            em.wait("tensor", bank_free[bank])
                            if qj == 0:
                                em.wait("tensor", tok["gelu_last"])
                        if a == 0 and qj == 0:
                            em.wait("tensor", tok_ld)

                        def fmm(e, nci=nci, qj=qj, sl=sl, a=a, bank=bank, first=first):
                            return e.matmul(
                                PS[bank][:], G[nci][:, 128 * qj:128 * (qj + 1)],
                                w2slot[sl][:, 512 * a:512 * (a + 1)],
                                start=first, stop=False)
                        if a == 3 and qj == 3:
                            last_grp = em.op("tensor", fmm, inc=pe)
                        else:
                            em.op("tensor", fmm)
                w2_free[sl] = last_grp
                last_pe = last_grp
            for qj in range(4):
                bank = quad[qj]
                tk = em.op("tensor", lambda e, hs=hs, bank=bank: e.matmul(
                    PS[bank][:], onesr[0:1, 0:128],
                    b1r[0:1, 512 * hs:512 * (hs + 1)], start=False, stop=True), inc=pe)
                bank_ready[bank] = tk
                last_pe = tk
                s2i = (hs * 4 + qj) % 2
                em.wait("scalar", tk)
                em.wait("scalar", gp_free[s2i])
                tok_si = em.op("scalar", lambda e, s2i=s2i, bank=bank: e.activation(
                    gp[s2i][:], PS[bank][:], AF.Silu), inc=sc)
                bank_free[bank] = tok_si
                em.wait("scalar", xp_free[s2i])
                tok_x = em.op("scalar", lambda e, seq=seq, r=row0 + 128 * qj, hs=hs,
                              s2i=s2i: e.dma_start(
                    xp[s2i][:], x_in.ap()[seq, r:r + 128, 512 * hs:512 * (hs + 1)]),
                    inc=dA, amt=16)
                em.wait("vector", tok_si)
                em.wait("vector", tok["sch_ready"])
                em.wait("vector", tp_free[s2i])
                em.op("vector", lambda e, s2i=s2i, qj=qj, hs=hs: e.tensor_tensor(
                    tp[s2i][:], gp[s2i][:],
                    s_ch[:, 2048 * qj + 512 * hs:2048 * qj + 512 * (hs + 1)], ALU.mult))
                em.wait("vector", tok_x)
                tok_cb = em.op("vector", lambda e, s2i=s2i: e.tensor_tensor(
                    tp[s2i][:], tp[s2i][:], xp[s2i][:], ALU.add), inc=ve)
                gp_free[s2i] = tok_cb
                xp_free[s2i] = tok_cb
                em.wait("scalar", tok_cb)
                tok_st = em.op("scalar", lambda e, seq=seq, r=row0 + 128 * qj, hs=hs,
                               s2i=s2i: e.dma_start(
                    y_out.ap()[seq, r:r + 128, 512 * hs:512 * (hs + 1)], tp[s2i][:]),
                    inc=dA, amt=16)
                tp_free[s2i] = tok_st
                tok["out_last"] = tok_st
        tok["ffn2_pe_done"] = last_pe
        tok["sch_ve_done"] = tok_cb

    wready["qkv"] = prol_transpose(wqkv_bf, wqkvT, 2560, 16, tok_c_qkv)
    wready["wo"] = prol_transpose(wo_bf, woT, 2048, 16, tok_c_wo, store_rows=wo_rows)
    phase_A1(0)
    phase_A2(0)
    phase_B(0)
    phase_C(0)
    phase_D(0)
    wready["w0"] = prol_transpose(w0_bf, w0T, NF, 16, tok_c_w0)
    phase_E(0)
    wready["w1"] = prol_transpose(w1_bf, w1T, 2048, 64, tok_c_w1)
    phase_F(0)
    for c in range(1, NCH):
        phase_A1(c)
        phase_A2(c)
        phase_B(c)
        phase_C(c)
        phase_D(c)
        phase_E(c)
        phase_F(c)

    em.wait("sync", tok["out_last"])
    em.wait("sync", (dA, dA.v))
    em.wait("sync", (dS, dS.v))
    em.wait("sync", (dP, dP.v))

    with nc.Block() as block:
        @block.sync
        def _(e):
            for f in em.q["sync"]:
                f(e)

        @block.tensor
        def _(e):
            for f in em.q["tensor"]:
                f(e)

        @block.vector
        def _(e):
            for f in em.q["vector"]:
                f(e)

        @block.scalar
        def _(e):
            for f in em.q["scalar"]:
                f(e)

        @block.gpsimd
        def _(e):
            for f in em.q["gpsimd"]:
                f(e)

    ctx.close()
    return nc


def _tables():
    inv = (1.0 / (10000.0 ** (np.arange(0, D, 2, dtype=np.float32) /
                              np.float32(D)))).astype(np.float32)
    pos = np.arange(S, dtype=np.float32)
    ang = (pos[None, :] * inv[:, None]).astype(np.float32)
    cos32 = np.cos(ang.astype(np.float64)).astype(np.float32)
    sin32 = np.sin(ang.astype(np.float64)).astype(np.float32)
    cosf = np.tile(cos32, (4, 1)).astype(bfdt)
    sinf = np.tile(sin32, (4, 1)).astype(bfdt)
    p = np.zeros((128, 128), np.float32)
    for bb in (0, 64):
        for r in range(32):
            p[bb + r + 32, bb + r] = -1.0
            p[bb + r, bb + r + 32] = 1.0
    trimask = np.triu(np.ones((128, 128), np.float32)).astype(bfdt)
    idf = np.eye(64, dtype=np.float32)
    idb = np.eye(128, dtype=np.float32).astype(bfdt)
    return cosf, sinf, p.astype(bfdt), trimask, idf, idb


def kernel(L0_attn_hidden_states, L0_attn_w_qkv, L0_attn_w_o,
           L0_ffn_W0, L0_ffn_bias0, L0_ffn_W1, L0_ffn_bias1):
    x = np.ascontiguousarray(np.asarray(L0_attn_hidden_states, dtype=np.float32))
    cosf, sinf, p128t, trimask, idf, idb = _tables()
    shared = dict(
        wqkv=np.ascontiguousarray(np.asarray(L0_attn_w_qkv, np.float32)),
        wo=np.ascontiguousarray(np.asarray(L0_attn_w_o, np.float32)),
        w0=np.ascontiguousarray(np.asarray(L0_ffn_W0, np.float32)),
        b0=np.ascontiguousarray(np.asarray(L0_ffn_bias0, np.float32)),
        w1=np.ascontiguousarray(np.asarray(L0_ffn_W1, np.float32)),
        b1=np.ascontiguousarray(np.asarray(L0_ffn_bias1, np.float32)),
        tcos=cosf, tsin=sinf, tp128=p128t, ttrim=trimask, tidf=idf, tidb=idb,
    )
    nc = build_program()
    in_maps = []
    for i in range(NCORES):
        m = dict(shared)
        m["x"] = np.ascontiguousarray(x[BLOC * i:BLOC * (i + 1)])
        in_maps.append(m)
    res = run_bass_kernel_spmd(nc, in_maps, core_ids=list(range(NCORES)))
    return np.concatenate([np.asarray(r["y"]) for r in res.results], axis=0)
